# version 01
# baseline (speedup 1.0000x reference)
import numpy as np
import ml_dtypes
import concourse.bass as bass
import concourse.mybir as mybir
from concourse.bass_utils import run_bass_kernel_spmd
from concourse.bass_core_rust import IndirectOffsetOnAxis
from contextlib import ExitStack

F32 = mybir.dt.float32
BF16 = mybir.dt.bfloat16
U32 = mybir.dt.uint32
AF = mybir.ActivationFunctionType
ALU = mybir.AluOpType
AX = mybir.AxisListType

T, D = 4096, 1024
SC = 256
NSC = T // SC
CPS = SC // 128
NCH = T // 128
NT = T // 128
E, CAP, FF = 8, 640, 4096
NTOT = E * CAP
ENGINES = ("sync", "gpsimd", "tensor", "vector", "scalar")
ESEM = {"tensor": "t", "vector": "v", "scalar": "s", "gpsimd": "g"}
KERNEL_N_SUB = 4
DBG = {"nsc": None, "scan": True, "dump": False, "chunks": None, "sdump": False, "sstage": 9,
       "bonly": False, "bnc": None, "bp2": True, "bstage": 9, "belu": True, "bp2s": 9}


class P:
    NL = 6

    def __init__(self, nc, ctx):
        self.nc = nc
        self.ops = {e: [] for e in ENGINES}
        self.semh = {}
        self.val = {}
        self.last_wait = {e: {} for e in ENGINES}
        names = ["t", "v", "s", "g"]
        names += [f"dS{i}" for i in range(self.NL)] + [f"dG{i}" for i in range(self.NL)]
        for snm in names:
            self.semh[snm] = ctx.enter_context(nc.semaphore(snm))
            self.val[snm] = 0
        self.dma_idx = {"sync": 0, "gpsimd": 0}

    def emit(self, eng, fn, waits=(), dma=False, sep=False):
        waits = list(waits)
        if dma:
            q = "S" if eng == "sync" else "G"
            lane = self.dma_idx[eng] % self.NL
            self.dma_idx[eng] += 1
            sig = f"d{q}{lane}"
            amt = 16
            waits.append((sig, self.val[sig], None))
        else:
            sig, amt = ESEM[eng], 1
        w = []
        for ev in waits:
            if ev is None:
                continue
            s, v, peng = ev
            if v <= 0:
                continue
            if self.last_wait[eng].get(s, 0) >= v:
                continue
            self.last_wait[eng][s] = v
            w.append((s, v))
        self.val[sig] += amt
        self.ops[eng].append((w, fn, sig, amt, sep))
        return (sig, self.val[sig], None if dma else eng)

    def snap(self):
        return [(s, self.val[s], None) for s in self.val]

    def finalize(self):
        semh = self.semh

        def mk(lst):
            def body(eng):
                for w, fn, sig, amt, sep in lst:
                    for s, v in w:
                        eng.wait_ge(semh[s], v)
                    if sep:
                        eng.nop()
                    inst = fn(eng)
                    if sig is not None:
                        inst.then_inc(semh[sig], amt)
            return body

        with self.nc.Block() as block:
            for e in ENGINES:
                getattr(block, e)(mk(self.ops[e]))


def build_program(n_sub=4):
    nc = bass.Bass("TRN2", target_bir_lowering=False)
    I = {}
    for nm, shp, dt in (
        ("x_in", [T, D], F32),
        ("wqaT", [D, D], F32), ("wkaT", [D, D], F32), ("wvaT", [D, D], F32),
        ("woaT", [D, D], F32), ("wbaT", [D, 16], F32),
        ("wqbN", [D, D], F32), ("wkbN", [D, D], F32), ("wvbT", [D, D], F32),
        ("projb", [D, 256], F32), ("w_router", [D, 8], F32),
        ("wi", [E, D, FF], F32), ("wo", [E, FF, D], F32),
        ("convw", [128, 96], F32), ("onw_rep", [128, 1], F32),
        ("ident_bf", [128, 128], BF16), ("ident_f", [128, 128], F32),
        ("nmask_sl", [128, 128], BF16), ("nmask_su", [128, 128], BF16),
        ("mask_iu", [128, 128], BF16),
        ("iota8", [128, 8], F32), ("ones8col", [8, 1], F32),
        ("zeros128", [128, 512], F32), ("selm", [8, 32], F32),
        ("onesf_c", [128, 1], F32),
        ("obk16", [128, 128], BF16), ("selc", [128, 1024], BF16),
    ):
        I[nm] = nc.dram_tensor(nm, shp, dt, kind="ExternalInput").ap()

    out = nc.dram_tensor("out", [T, D], F32, kind="ExternalOutput").ap()
    dbg = nc.dram_tensor("dbg", [128, 16384], BF16, kind="ExternalOutput").ap()
    xr_a = nc.dram_tensor("xr_a", [T, D], F32, kind="Internal").ap()
    xr_b = nc.dram_tensor("xr_b", [T, D], F32, kind="Internal").ap()
    y_dram = nc.dram_tensor("y_dram", [T, D], BF16, kind="Internal").ap()
    bkt_dram = nc.dram_tensor("bkt_dram", [NTOT, D], BF16, kind="Internal").ap()
    out2_dram = nc.dram_tensor("out2_dram", [NTOT, D], BF16, kind="Internal").ap()
    cc_in = nc.dram_tensor("cc_in", [1, 16], F32, kind="Internal").ap()
    cc_out = nc.dram_tensor("cc_out", [8, 16], F32, kind="Internal").ap()

    ctx = ExitStack()
    sb = lambda n, s, d: ctx.enter_context(nc.sbuf_tensor(n, s, d))
    psum = lambda n, s, d: ctx.enter_context(nc.psum_tensor(n, s, d))
    p = P(nc, ctx)

    ident = sb("identS", [128, 128], BF16)
    identf = sb("identfS", [128, 128], F32)
    nm_sl = sb("nm_slS", [128, 128], BF16)
    nm_su = sb("nm_suS", [128, 128], BF16)
    m_iu = sb("m_iuS", [128, 128], BF16)
    iota8 = sb("iota8S", [128, 8], F32)
    ones8c = sb("ones8cS", [8, 1], F32)
    onwrep = sb("onwrepS", [128, 1], F32)
    zer = sb("zerS", [128, 512], F32)
    zerbf = sb("zerbfS", [128, 512], BF16)
    convw = sb("convwS", [128, 96], F32)
    selm = sb("selmS", [128, 32], F32)
    onesf = sb("onesfS", [128, 1], F32)
    onesbf = sb("onesbfS", [128, 64], BF16)
    obk16 = sb("obk16S", [128, 128], BF16)
    selc = sb("selcS", [128, 1024], BF16)

    bank = [psum(f"bank{i}", [128, 512], F32) for i in range(6)]
    bankT = psum("bankT", [128, 1024], BF16)
    bankS = psum("bankS", [128, 512], F32)

    for dst, src in ((ident[:, :], I["ident_bf"]), (identf[:, :], I["ident_f"]),
                     (nm_sl[:, :], I["nmask_sl"]), (nm_su[:, :], I["nmask_su"]),
                     (m_iu[:, :], I["mask_iu"]), (iota8[:, :], I["iota8"]),
                     (ones8c[:, :], I["ones8col"]), (onwrep[:, :], I["onw_rep"]),
                     (zer[:, :], I["zeros128"]), (convw[:, :], I["convw"]),
                     (selm[0:8, :], I["selm"]), (obk16[:, :], I["obk16"]),
                     (onesf[:, :], I["onesf_c"]),
                     (selc[:, :], I["selc"])):
        p.emit("sync", lambda e, d=dst, s=src: e.dma_start(out=d, in_=s), dma=True)
    consts_evs = [(f"dS{i}", p.val[f"dS{i}"], None) for i in range(P.NL)]
    zb_ev = p.emit("vector", lambda e: e.tensor_copy(out=zerbf[:, :], in_=zer[:, :]),
                   waits=consts_evs)
    p.emit("vector", lambda e: e.memset(onesbf[:, :], 1.0), waits=consts_evs)
    init_ev = p.snap()

    def weight_cast(wt_dram, wsb_f32, wsb_bf, nchunks, ncols, waits=(), scale_ap=None):
        prev = [None, None]
        last = None
        for c in range(nchunks):
            h = 0
            lev = p.emit("sync", lambda e, c=c, h=h, wt=wt_dram, st=wsb_f32, k=ncols:
                         e.dma_start(out=st[:, h * k:(h + 1) * k],
                                     in_=wt[c * 128:(c + 1) * 128, 0:k]),
                         waits=list(waits) + [prev[h]], dma=True)
            if scale_ap is None:
                fn = lambda e, c=c, h=h, st=wsb_f32, bfb=wsb_bf, k=ncols: e.tensor_copy(
                    out=bfb[:, c * k:(c + 1) * k], in_=st[:, h * k:(h + 1) * k])
            else:
                fn = lambda e, c=c, h=h, st=wsb_f32, bfb=wsb_bf, k=ncols, sa=scale_ap: \
                    e.tensor_scalar(out=bfb[:, c * k:(c + 1) * k],
                                    in0=st[:, h * k:(h + 1) * k],
                                    scalar1=sa, scalar2=None, op0=ALU.mult)
            last = p.emit("vector", fn, waits=[lev])
            prev[h] = last
        return last

    def dr(fn):
        def g(e):
            fn(e)
            return e.drain()
        return g

    def ln_tile(x_ap, stat, y_out_ap, sq_scr, waits, prev_ln=None):
        if prev_ln is None:
            prev_ln = []
        elif not isinstance(prev_ln, list):
            prev_ln = [prev_ln]
        w0 = list(waits) + prev_ln
        a = p.emit("scalar", lambda e: e.activation(
            out=sq_scr, in_=x_ap, func=AF.Square, accum_out=stat[:, 2:3]), waits=w0)
        b = p.emit("vector", lambda e: e.reduce_sum(
            out=stat[:, 0:2].unsqueeze(2), in_=x_ap.rearrange("p (g f) -> p g f", g=2),
            axis=AX.X), waits=w0)
        c1 = p.emit("gpsimd", lambda e: e.tensor_tensor(
            out=stat[:, 3:4], in0=stat[:, 0:1], in1=stat[:, 1:2], op=ALU.add), waits=[b])
        c2 = p.emit("vector", lambda e: e.tensor_scalar(
            out=stat[:, 4:5], in0=stat[:, 3:4], scalar1=1.0 / D, scalar2=None,
            op0=ALU.mult), waits=[c1])
        c3 = p.emit("gpsimd", lambda e: e.tensor_scalar(
            out=stat[:, 5:6], in0=stat[:, 2:3], scalar1=1.0 / D, scalar2=None,
            op0=ALU.mult), waits=[a])
        c4 = p.emit("gpsimd", lambda e: e.tensor_tensor(
            out=stat[:, 6:7], in0=stat[:, 4:5], in1=stat[:, 4:5], op=ALU.mult), waits=[c2])
        c5 = p.emit("vector", lambda e: e.scalar_tensor_tensor(
            out=stat[:, 5:6], in0=stat[:, 5:6], scalar=1e-5, in1=stat[:, 6:7],
            op0=ALU.add, op1=ALU.subtract), waits=[c3, c4])
        c6 = p.emit("scalar", lambda e: e.activation(
            out=stat[:, 5:6], in_=stat[:, 5:6], func=AF.Sqrt), waits=[c5])
        c7 = p.emit("vector", lambda e: e.reciprocal(out=stat[:, 6:7], in_=stat[:, 5:6]),
                    waits=[c6])
        c8 = p.emit("gpsimd", lambda e: e.tensor_tensor(
            out=stat[:, 7:8], in0=stat[:, 4:5], in1=stat[:, 6:7], op=ALU.mult), waits=[c7])
        c9 = p.emit("vector", lambda e: e.tensor_scalar(
            out=stat[:, 7:8], in0=stat[:, 7:8], scalar1=-1.0, scalar2=None, op0=ALU.mult),
            waits=[c8])
        return p.emit("scalar", lambda e: e.activation(
            out=y_out_ap, in_=x_ap, func=AF.Identity, bias=stat[:, 7:8],
            scale=stat[:, 6:7]), waits=[c9])

    def tm_to_fm(src_cols_fn, dst_ap, waits):
        tevs = []
        for c in range(8):
            tevs.append(p.emit("tensor", lambda e, c=c: e.matmul(
                bankT[:, c * 128:(c + 1) * 128], src_cols_fn(c), ident[:, :],
                is_transpose=True, start=(c == 0), stop=(c == 7)),
                waits=waits))
        return p.emit("vector", lambda e: e.tensor_copy(
            out=dst_ap, in_=bankT[:, :].rearrange("p (c t) -> p c t", c=8)),
            waits=tevs)

    cc_prev = []
    _breg = {}

    def bc_reg(e):
        if "r" not in _breg:
            _breg["r"] = e.to_reg(NTOT - 1)
        return _breg["r"]

    def emit_attn_a(x_src, x_dst):
        s1 = ExitStack()
        s2 = lambda n, s, d: s1.enter_context(nc.sbuf_tensor(n, s, d))
        wstage = s2("wstage1", [128, 1024], F32)
        wqa = s2("wqa", [128, 8 * 1024], BF16)
        wka = s2("wka", [128, 8 * 1024], BF16)
        wva = s2("wva", [128, 8 * 1024], BF16)
        woa = s2("woa", [128, 8 * 1024], BF16)
        wba = s2("wba", [128, 8 * 16], BF16)
        wq_ev = weight_cast(I["wqaT"], wstage, wqa, 8, 1024, waits=consts_evs)
        wk_ev = weight_cast(I["wkaT"], wstage, wka, 8, 1024, waits=[wq_ev])
        wv_ev = weight_cast(I["wvaT"], wstage, wva, 8, 1024, waits=[wk_ev])
        wo_ev = weight_cast(I["woaT"], wstage, woa, 8, 1024, waits=[wv_ev],
                            scale_ap=onwrep[:, 0:1])
        wb_ev = weight_cast(I["wbaT"], wstage, wba, 8, 16, waits=[wo_ev])

        x_t = s2("x_t1", [128, 1024], F32)
        stat = s2("stat1", [128, 8], F32)
        sq32 = s2("sq32_1", [128, 1024], BF16)
        ytm = s2("ytm1", [128, 1024], BF16)
        yfm = s2("yfm1", [128, 8 * SC], BF16)
        raw = s2("raw1", [128, 8 * (SC + 3)], BF16)
        haloq = s2("haloq", [128, 24], BF16)
        halok = s2("halok", [128, 24], BF16)
        halov = s2("halov", [128, 24], BF16)
        qs_ = s2("qs1", [128, 8 * SC], BF16)
        ks_ = s2("ks1", [128, 8 * SC], BF16)
        vs_ = s2("vs1", [128, 8 * SC], BF16)
        diag = s2("diag1", [128, 4 * 128], BF16)
        fac = s2("fac1", [16, 3 * SC], F32)
        facb = s2("facb1", [128, 3 * SC], BF16)
        bfm = s2("bfm1", [16, SC], F32)
        btm = s2("btm1", [128, CPS * 16], F32)
        QFe = s2("QFe", [128, 8 * SC], BF16)
        QFo = s2("QFo", [128, 8 * SC], BF16)
        KFe = s2("KFe", [128, 8 * SC], BF16)
        KFo = s2("KFo", [128, 8 * SC], BF16)
        KBe = s2("KBe", [128, 8 * SC], BF16)
        KBo = s2("KBo", [128, 8 * SC], BF16)
        kcomb = s2("kcomb", [128, 8 * SC], BF16)
        Bp = [s2(f"Bp{j}", [128, 16 * 128], BF16) for j in range(7)]
        Cpa = s2("CpA", [128, 16 * 128], BF16)
        Cpb = s2("CpB", [128, 16 * 128], BF16)
        Ppm = s2("Ppm1", [128, 16 * 128], BF16)
        ytmp = s2("ytmp1", [128, 16 * 64], BF16)
        rhs_s = s2("rhs1", [128, 16 * 64], BF16)
        Ktm = s2("Ktm1", [128, 16 * 64], BF16)
        Vbt = s2("Vbt1", [128, 16 * 64], BF16)
        S0e = s2("S0e", [128, 512], BF16)
        S0o = s2("S0o", [128, 512], BF16)
        Ssb = s2("Ssb1", [64, 1024], F32)
        osc = s2("osc1", [128, 16 * 64], BF16)
        ofm = s2("ofm1", [128, 8 * 128], BF16)
        onrm = s2("onrm1", [128, 16], F32)
        xr2 = s2("xr2_1", [128, 1024], F32)
        xo_t = s2("xo_t1", [128, 1024], F32)

        def fmsl(te, to, h, c0, w):
            t = te if h % 2 == 0 else to
            return t[0:128, (h // 2) * SC + c0: (h // 2) * SC + c0 + w]

        def tmh(tsr, h):
            return tsr[:, h * 64:(h + 1) * 64]

        def s0sl(h):
            t = S0e if h % 2 == 0 else S0o
            return t[0:128, (h // 2) * 64:(h // 2) * 64 + 64]

        p.emit("vector", lambda e: e.memset(Ssb[:, :], 0.0), waits=init_ev)
        p.emit("vector", lambda e: e.memset(raw[:, 0:8 * (SC + 3)], 0.0), waits=init_ev)
        p.emit("vector", lambda e: e.memset(haloq[:, :], 0.0), waits=init_ev)
        p.emit("vector", lambda e: e.memset(halok[:, :], 0.0), waits=init_ev)
        p.emit("vector", lambda e: e.memset(halov[:, :], 0.0), waits=init_ev)
        for t_ in (QFe, KFe, KBe):
            p.emit("vector", lambda e, t_=t_: e.memset(t_[64:128, :], 0.0), waits=init_ev)
        for t_ in (QFo, KFo, KBo):
            p.emit("vector", lambda e, t_=t_: e.memset(t_[0:64, :], 0.0), waits=init_ev)
        p.emit("vector", lambda e: e.memset(S0e[64:128, :], 0.0), waits=init_ev)
        p.emit("vector", lambda e: e.memset(S0o[0:64, :], 0.0), waits=init_ev)
        p.emit("vector", lambda e: e.memset(facb[:, :], 0.0), waits=init_ev)
        barrier = p.snap()

        nsc_run = DBG["nsc"] if DBG["nsc"] is not None else NSC
        for sc in range(nsc_run):
            yevs = []
            prev_ln = None
            x_buf_ev = [None, None]
            for it in range(CPS):
                tt = sc * SC + it * 128
                xa = x_t[:, 0:1024]
                lx = p.emit("sync", lambda e, xa=xa, tt=tt: e.dma_start(
                    out=xa, in_=x_src[tt:tt + 128, :]),
                    waits=barrier + [x_buf_ev[0]], dma=True)
                yev = ln_tile(xa, stat, ytm[:, :], sq32[:, :], [lx], prev_ln)
                x_buf_ev[0] = yev
                cp = tm_to_fm(lambda c: ytm[:, c * 128:(c + 1) * 128],
                              yfm[:, :].rearrange("p (c t) -> p c t", c=8)
                              [:, :, it * 128:(it + 1) * 128],
                              [yev] + barrier + (yevs[-1:] if yevs else []))
                prev_ln = cp
                yevs.append(cp)
            y_all = list(yevs)

            bmm = []
            for c in range(8):
                bmm.append(p.emit("tensor", lambda e, c=c: e.matmul(
                    bank[5][0:16, 0:SC], wba[:, c * 16:(c + 1) * 16],
                    yfm[:, c * SC:(c + 1) * SC], start=(c == 0), stop=(c == 7)),
                    waits=y_all + [wb_ev] + barrier))
            bsig = p.emit("scalar", lambda e: e.activation(
                out=bfm[:, :], in_=bank[5][0:16, 0:SC], func=AF.Sigmoid), waits=bmm)
            btevs = []
            for it in range(CPS):
                tr = p.emit("tensor", lambda e, it=it: e.matmul(
                    bank[4][:, 0:16], bfm[:, it * 128:(it + 1) * 128],
                    identf[0:16, 0:16], is_transpose=True, start=True, stop=True),
                    waits=[bsig] + barrier + btevs[-1:])
                cpb = p.emit("vector", lambda e, it=it: e.tensor_copy(
                    out=btm[:, it * 16:(it + 1) * 16], in_=bank[4][:, 0:16]),
                    waits=[tr])
                btevs.append(cpb)

            sil = {"q": [], "k": [], "v": []}
            prev_halo = []
            for tname, wsb, wev, dst, cwoff, halo in (
                    ("q", wqa, wq_ev, qs_, 0, haloq), ("k", wka, wk_ev, ks_, 32, halok),
                    ("v", wva, wv_ev, vs_, 64, halov)):
                raw_evs = []
                for c in range(8):
                    mms = []
                    for ic in range(8):
                        mms.append(p.emit("tensor", lambda e, c=c, ic=ic, wsb=wsb:
                                          e.matmul(
                            bank[c % 2][:, 0:SC],
                            wsb[:, ic * 1024 + c * 128: ic * 1024 + (c + 1) * 128],
                            yfm[:, ic * SC:(ic + 1) * SC],
                            start=(ic == 0), stop=(ic == 7)),
                            waits=y_all + [wev] + barrier + prev_halo
                            + (raw_evs[c - 2:c - 1] if c >= 2 else [])))
                    rev = p.emit("scalar", lambda e, c=c: e.copy(
                        out=raw[:, c * (SC + 3) + 3: (c + 1) * (SC + 3)],
                        in_=bank[c % 2][:, 0:SC]), waits=mms)
                    raw_evs.append(rev)
                hrest = []
                for c in range(8):
                    hrest.append(p.emit("vector", lambda e, c=c, halo=halo: e.tensor_copy(
                        out=raw[:, c * (SC + 3):c * (SC + 3) + 3],
                        in_=halo[:, c * 3:(c + 1) * 3]), waits=[raw_evs[c]] + prev_halo))
                dlast = []
                for c in range(8):
                    devs = []
                    for k in range(4):
                        devs.append(p.emit("vector", lambda e, c=c, k=k, cwoff=cwoff:
                                           e.tensor_scalar(
                            out=diag[:, k * 128:(k + 1) * 128], in0=ident[:, :],
                            scalar1=convw[:, cwoff + c * 4 + k: cwoff + c * 4 + k + 1],
                            scalar2=None, op0=ALU.mult), waits=[hrest[c]] + dlast))
                    cm = []
                    for k in range(4):
                        cm.append(p.emit("tensor", lambda e, c=c, k=k: e.matmul(
                            bank[2 + c % 2][:, 0:SC], diag[:, k * 128:(k + 1) * 128],
                            raw[:, c * (SC + 3) + k: c * (SC + 3) + k + SC],
                            start=(k == 0), stop=(k == 3)),
                            waits=[devs[k], raw_evs[c], hrest[c]]
                            + (sil[tname][c - 2:c - 1] if c >= 2 else [])))
                    dlast = [cm[-1]]
                    sev = p.emit("scalar", lambda e, c=c, dst=dst: e.activation(
                        out=dst[:, c * SC:(c + 1) * SC], in_=bank[2 + c % 2][:, 0:SC],
                        func=AF.Silu), waits=cm)
                    sil[tname].append(sev)
                prev_halo = []
                for c in range(8):
                    prev_halo.append(p.emit("vector",
                                            lambda e, c=c, halo=halo: e.tensor_copy(
                        out=halo[:, c * 3:(c + 1) * 3],
                        in_=raw[:, c * (SC + 3) + SC: c * (SC + 3) + SC + 3]),
                        waits=[sil[tname][c], dlast[0]]))

            sq_done = []
            for tname, src, off in (("q", qs_, 0), ("k", ks_, SC)):
                smms = []
                sqp = None
                for c in range(8):
                    sq = p.emit("scalar", lambda e, c=c, src=src: e.activation(
                        out=sq32[:, 0:SC], in_=src[:, c * SC:(c + 1) * SC],
                        func=AF.Square), waits=[sil[tname][c], sqp] + smms[-1:])
                    smms.append(p.emit("tensor", lambda e, c=c: e.matmul(
                        bank[4][0:16, 0:SC], obk16[:, c * 16:(c + 1) * 16],
                        sq32[:, 0:SC], start=(c == 0), stop=(c == 7)),
                        waits=[sq] + btevs + sq_done))
                    sqp = sq
                sq_done.append(p.emit("scalar", lambda e, off=off: e.copy(
                    out=fac[:, off:off + SC], in_=bank[4][0:16, 0:SC]), waits=smms))
            f1 = p.emit("scalar", lambda e: e.activation(
                out=fac[:, 0:2 * SC], in_=fac[:, 0:2 * SC], func=AF.Sqrt), waits=sq_done)
            f2 = p.emit("gpsimd", lambda e: e.tensor_scalar(
                out=fac[:, 0:2 * SC], in0=fac[:, 0:2 * SC], scalar1=1e-6, scalar2=None,
                op0=ALU.max), waits=[f1])
            f3 = p.emit("vector", lambda e: e.reciprocal(
                out=fac[:, 0:2 * SC], in_=fac[:, 0:2 * SC]), waits=[f2])
            f4 = p.emit("gpsimd", lambda e: e.tensor_tensor(
                out=fac[:, 2 * SC:3 * SC], in0=fac[:, SC:2 * SC], in1=bfm[:, :],
                op=ALU.mult), waits=[f3, bsig])
            f5 = p.emit("gpsimd", lambda e: e.tensor_scalar(
                out=fac[:, 0:SC], in0=fac[:, 0:SC], scalar1=0.125, scalar2=None,
                op0=ALU.mult), waits=[f3])
            fcst = p.emit("vector", lambda e: e.tensor_copy(
                out=facb[0:16, :], in_=fac[:, :]), waits=[f4, f5])
            fd = [fcst]

            prev_b4 = sq_done[-1:] + btevs
            for c in range(8):
                for r, srcs, dste, dsto, siln in (
                        (0, qs_, QFe, QFo, "q"), (32, ks_, KFe, KFo, "k"),
                        (64, ks_, KBe, KBo, "k")):
                    lo = (r // 32) * SC
                    m1 = p.emit("tensor", lambda e, c=c, lo=lo: e.matmul(
                        bank[4][:, 0:SC], selc[:, c * 128:(c + 1) * 128],
                        facb[0:128, lo:lo + SC],
                        start=True, stop=True), waits=fd + prev_b4)
                    se_ = p.emit("vector", lambda e, c=c, srcs=srcs, dste=dste:
                                 e.tensor_tensor(
                        out=dste[0:64, c * SC:(c + 1) * SC],
                        in0=srcs[0:64, c * SC:(c + 1) * SC],
                        in1=bank[4][0:64, 0:SC], op=ALU.mult),
                        waits=[m1] + sil[siln])
                    so_ = p.emit("vector", lambda e, c=c, srcs=srcs, dsto=dsto:
                                 e.tensor_tensor(
                        out=dsto[64:128, c * SC:(c + 1) * SC],
                        in0=srcs[64:128, c * SC:(c + 1) * SC],
                        in1=bank[4][64:128, 0:SC], op=ALU.mult),
                        waits=[m1] + sil[siln])
                    if dste is KFe:
                        kce = p.emit("vector", lambda e, c=c: e.tensor_copy(
                            out=kcomb[0:64, c * SC:(c + 1) * SC],
                            in_=KFe[0:64, c * SC:(c + 1) * SC]), waits=[se_])
                        kco = p.emit("vector", lambda e, c=c: e.tensor_copy(
                            out=kcomb[64:128, c * SC:(c + 1) * SC],
                            in_=KFo[64:128, c * SC:(c + 1) * SC]), waits=[so_])
                        prev_b4 = [se_, so_, kce, kco]
                    else:
                        prev_b4 = [se_, so_]

            ch_run = DBG["chunks"] if DBG["chunks"] is not None else CPS
            for it in range(ch_run):
                ci = sc * CPS + it
                tc = it * 128
                cbar = p.snap()

                vt = []
                for c in range(8):
                    vt.append(p.emit("tensor", lambda e, c=c, tc=tc: e.matmul(
                        bankT[:, c * 128:(c + 1) * 128],
                        vs_[:, c * SC + tc: c * SC + tc + 128], ident[:, :],
                        is_transpose=True, start=(c == 0), stop=(c == 7)),
                        waits=cbar))
                vb = p.emit("vector", lambda e, it=it: e.tensor_tensor(
                    out=Vbt[:, :].rearrange("p (h d) -> p h d", h=16),
                    in0=bankT[:, :].rearrange("p (h d) -> p h d", h=16),
                    in1=btm[:, it * 16:(it + 1) * 16].unsqueeze(2)
                    .broadcast_to([128, 16, 64]), op=ALU.mult), waits=vt)
                kt = []
                for c in range(8):
                    kt.append(p.emit("tensor", lambda e, c=c, tc=tc: e.matmul(
                        bankT[:, c * 128:(c + 1) * 128],
                        kcomb[:, c * SC + tc: c * SC + tc + 128], ident[:, :],
                        is_transpose=True, start=(c == 0), stop=(c == 7)),
                        waits=[vb] + cbar))
                ktc = p.emit("scalar", lambda e: e.copy(out=Ktm[:, :], in_=bankT[:, :]),
                             waits=kt)

                s0ce = p.emit("scalar", lambda e: e.copy(
                    out=S0e[0:64, :],
                    in_=Ssb[:, :].rearrange("p (h x) -> p h x", h=16)[:, 0::2, :]),
                    waits=cbar)
                s0co = p.emit("scalar", lambda e: e.copy(
                    out=S0o[64:128, :],
                    in_=Ssb[:, :].rearrange("p (h x) -> p h x", h=16)[:, 1::2, :]),
                    waits=cbar + [s0ce])
                s0c = [s0ce, s0co]

                g1 = []
                for h in range(16):
                    g1.append(p.emit("tensor", lambda e, h=h, tc=tc: e.matmul(
                        bank[h // 4][:, (h % 4) * 128:(h % 4) * 128 + 128],
                        fmsl(KFe, KFo, h, tc, 128), fmsl(KBe, KBo, h, tc, 128),
                        start=(h % 4 == 0), stop=(h % 4 == 3)), waits=cbar))
                g1e = []
                for g in range(4):
                    g1e.append(p.emit("vector", lambda e, g=g: e.tensor_tensor(
                        out=Bp[0][:, g * 512:(g + 1) * 512]
                        .rearrange("p (h t) -> p h t", h=4),
                        in0=bank[g][:, 0:512].rearrange("p (h t) -> p h t", h=4),
                        in1=nm_su[:, :].unsqueeze(1).broadcast_to([128, 4, 128]),
                        op=ALU.mult), waits=g1[g * 4:g * 4 + 4]))
                g2 = []
                for h in range(16):
                    g2.append(p.emit("tensor", lambda e, h=h, tc=tc: e.matmul(
                        bank[h // 4][:, (h % 4) * 128:(h % 4) * 128 + 128],
                        fmsl(KBe, KBo, h, tc, 128), fmsl(KFe, KFo, h, tc, 128),
                        start=(h % 4 == 0), stop=(h % 4 == 3)), waits=g1e))
                g2e = []
                for g in range(4):
                    g2e.append(p.emit("vector", lambda e, g=g: e.tensor_tensor(
                        out=Cpa[:, g * 512:(g + 1) * 512]
                        .rearrange("p (h t) -> p h t", h=4),
                        in0=bank[g][:, 0:512].rearrange("p (h t) -> p h t", h=4),
                        in1=nm_sl[:, :].unsqueeze(1).broadcast_to([128, 4, 128]),
                        op=ALU.mult), waits=g2[g * 4:g * 4 + 4]))

                Ccur, Cnext = Cpa, Cpb
                c_ready = list(g2e)
                b_ready = {0: list(g1e)}
                for j in range(6):
                    bm = []
                    for h in range(16):
                        bm.append(p.emit("tensor", lambda e, h=h, j=j, Cc=Ccur: e.matmul(
                            bank[h // 4][:, (h % 4) * 128:(h % 4) * 128 + 128],
                            Cc[:, h * 128:(h + 1) * 128],
                            Bp[j][:, h * 128:(h + 1) * 128],
                            start=(h % 4 == 0), stop=(h % 4 == 3)),
                            waits=c_ready + b_ready[j]))
                    be = []
                    for g in range(4):
                        if g % 2 == 0:
                            be.append(p.emit("scalar", lambda e, g=g, j=j: e.copy(
                                out=Bp[j + 1][:, g * 512:(g + 1) * 512],
                                in_=bank[g][:, 0:512]), waits=bm[g * 4:g * 4 + 4]))
                        else:
                            be.append(p.emit("vector", lambda e, g=g, j=j: e.tensor_copy(
                                out=Bp[j + 1][:, g * 512:(g + 1) * 512],
                                in_=bank[g][:, 0:512]), waits=bm[g * 4:g * 4 + 4]))
                    b_ready[j + 1] = be
                    if j < 5:
                        cmm = []
                        for h in range(16):
                            cmm.append(p.emit("tensor",
                                              lambda e, h=h, j=j, Cc=Ccur: e.matmul(
                                bank[h // 4][:, (h % 4) * 128:(h % 4) * 128 + 128],
                                Bp[j][:, h * 128:(h + 1) * 128],
                                Cc[:, h * 128:(h + 1) * 128],
                                start=(h % 4 == 0), stop=(h % 4 == 3)),
                                waits=be + c_ready))
                        ce = []
                        for g in range(4):
                            if g % 2 == 0:
                                ce.append(p.emit("vector",
                                                 lambda e, g=g, Cn=Cnext: e.tensor_copy(
                                    out=Cn[:, g * 512:(g + 1) * 512],
                                    in_=bank[g][:, 0:512]), waits=cmm[g * 4:g * 4 + 4]))
                            else:
                                ce.append(p.emit("scalar",
                                                 lambda e, g=g, Cn=Cnext: e.copy(
                                    out=Cn[:, g * 512:(g + 1) * 512],
                                    in_=bank[g][:, 0:512]), waits=cmm[g * 4:g * 4 + 4]))
                        c_ready = ce
                        Ccur, Cnext = Cnext, Ccur

                km = []
                for h in range(16):
                    km.append(p.emit("tensor", lambda e, h=h, tc=tc: e.matmul(
                        bank[4 + h // 8][:, (h % 8) * 64:(h % 8) * 64 + 64],
                        fmsl(KBe, KBo, h, tc, 128), s0sl(h),
                        start=(h % 8 == 0), stop=(h % 8 == 7)),
                        waits=s0c + cbar))
                rhs_e = []
                for g in range(2):
                    rhs_e.append(p.emit("vector", lambda e, g=g: e.tensor_tensor(
                        out=rhs_s[:, g * 512:(g + 1) * 512],
                        in0=Vbt[:, g * 512:(g + 1) * 512],
                        in1=bank[4 + g][:, 0:512], op=ALU.subtract),
                        waits=km + [vb]))
                ysrc = rhs_s
                y_ready = list(rhs_e)
                prev_ycopy = list(rhs_e)
                for idx, j in enumerate((6, 5, 4, 3, 2, 1, 0)):
                    am = []
                    yini2 = []
                    for g in range(2):
                        yini2.append(p.emit("tensor", lambda e, g=g, ysrc=ysrc: e.matmul(
                            bank[4 + g][:, 0:512], ident[:, :],
                            ysrc[:, g * 512:(g + 1) * 512], start=True, stop=False),
                            waits=y_ready + prev_ycopy))
                    for h in range(16):
                        am.append(p.emit("tensor",
                                         lambda e, h=h, j=j, ysrc=ysrc: e.matmul(
                            bank[4 + h // 8][:, (h % 8) * 64:(h % 8) * 64 + 64],
                            Bp[j][:, h * 128:(h + 1) * 128], tmh(ysrc, h),
                            start=False, stop=(h == 7 or h == 15)),
                            waits=y_ready + b_ready[j] + yini2))
                    yc = []
                    for g in range(2):
                        if idx % 2 == 0:
                            yc.append(p.emit("scalar", lambda e, g=g: e.copy(
                                out=ytmp[:, g * 512:(g + 1) * 512],
                                in_=bank[4 + g][:, 0:512]), waits=am[g * 8:g * 8 + 8]))
                        else:
                            yc.append(p.emit("vector", lambda e, g=g: e.tensor_copy(
                                out=ytmp[:, g * 512:(g + 1) * 512],
                                in_=bank[4 + g][:, 0:512]), waits=am[g * 8:g * 8 + 8]))
                    ysrc = ytmp
                    prev_ycopy = yc
                    y_ready = yc
                u_ready = y_ready

                su = []
                for h in range(16):
                    tgt = bankS if h % 2 == 0 else bank[5]
                    su.append(p.emit("tensor", lambda e, h=h, tgt=tgt: e.matmul(
                        tgt[0:64, (h // 2) * 64:(h // 2) * 64 + 64],
                        tmh(Ktm, h), tmh(ytmp, h),
                        start=(h < 2), stop=(h >= 14)),
                        waits=u_ready + [ktc] + s0c + cbar))
                sadd = []
                sadd.append(p.emit("vector", lambda e: e.tensor_tensor(
                    out=Ssb[:, :].rearrange("p (h x) -> p h x", h=16)[:, 0::2, :],
                    in0=Ssb[:, :].rearrange("p (h x) -> p h x", h=16)[:, 0::2, :],
                    in1=bankS[0:64, :].rearrange("p (h x) -> p h x", h=8),
                    op=ALU.add), waits=su + s0c))
                sadd.append(p.emit("vector", lambda e: e.tensor_tensor(
                    out=Ssb[:, :].rearrange("p (h x) -> p h x", h=16)[:, 1::2, :],
                    in0=Ssb[:, :].rearrange("p (h x) -> p h x", h=16)[:, 1::2, :],
                    in1=bank[5][0:64, :].rearrange("p (h x) -> p h x", h=8),
                    op=ALU.add), waits=su + s0c))

                pq = []
                for h in range(16):
                    pq.append(p.emit("tensor", lambda e, h=h, tc=tc: e.matmul(
                        bank[h // 4][:, (h % 4) * 128:(h % 4) * 128 + 128],
                        fmsl(KFe, KFo, h, tc, 128), fmsl(QFe, QFo, h, tc, 128),
                        start=(h % 4 == 0), stop=(h % 4 == 3)), waits=b_ready[6]))
                pqe = []
                for g in range(4):
                    pqe.append(p.emit("vector", lambda e, g=g: e.tensor_tensor(
                        out=Ppm[:, g * 512:(g + 1) * 512]
                        .rearrange("p (h t) -> p h t", h=4),
                        in0=bank[g][:, 0:512].rearrange("p (h t) -> p h t", h=4),
                        in1=m_iu[:, :].unsqueeze(1).broadcast_to([128, 4, 128]),
                        op=ALU.mult), waits=pq[g * 4:g * 4 + 4]))

                om = []
                for h in range(16):
                    om.append(p.emit("tensor", lambda e, h=h: e.matmul(
                        bank[h // 8][:, (h % 8) * 64:(h % 8) * 64 + 64],
                        Ppm[:, h * 128:(h + 1) * 128], tmh(ytmp, h),
                        start=(h % 8 == 0), stop=False), waits=pqe + u_ready))
                for h in range(16):
                    om.append(p.emit("tensor", lambda e, h=h, tc=tc: e.matmul(
                        bank[h // 8][:, (h % 8) * 64:(h % 8) * 64 + 64],
                        fmsl(QFe, QFo, h, tc, 128), s0sl(h),
                        start=False, stop=(h == 7 or h == 15)), waits=s0c))

                oq = []
                for g in range(2):
                    oq.append(p.emit("scalar", lambda e, g=g: e.activation(
                        out=sq32[:, g * 512:(g + 1) * 512], in_=bank[g][:, 0:512],
                        func=AF.Square), waits=om))
                ord_ = p.emit("vector", lambda e: e.reduce_sum(
                    out=onrm[:, :].unsqueeze(2),
                    in_=sq32[:, :].rearrange("p (h d) -> p h d", h=16), axis=AX.X),
                    waits=oq)
                on1 = p.emit("gpsimd", lambda e: e.tensor_scalar(
                    out=onrm[:, :], in0=onrm[:, :], scalar1=1.0 / 64, scalar2=1e-6,
                    op0=ALU.mult, op1=ALU.add), waits=[ord_])
                on2 = p.emit("vector", lambda e: e.reciprocal(
                    out=onrm[:, :], in_=onrm[:, :]), waits=[on1])
                on3 = p.emit("scalar", lambda e: e.activation(
                    out=onrm[:, :], in_=onrm[:, :], func=AF.Sqrt), waits=[on2])
                ose = []
                for g in range(2):
                    ose.append(p.emit("vector", lambda e, g=g: e.tensor_tensor(
                        out=osc[:, g * 512:(g + 1) * 512]
                        .rearrange("p (h d) -> p h d", h=8),
                        in0=bank[g][:, 0:512].rearrange("p (h d) -> p h d", h=8),
                        in1=onrm[:, g * 8:(g + 1) * 8].unsqueeze(2)
                        .broadcast_to([128, 8, 64]), op=ALU.mult), waits=[on3]))

                ocp = tm_to_fm(lambda c: osc[:, c * 128:(c + 1) * 128],
                               ofm[:, :].rearrange("p (c t) -> p c t", c=8),
                               ose + [ktc])
                lx2 = p.emit("sync", lambda e, ci=ci: e.dma_start(
                    out=xr2[:, :], in_=x_src[ci * 128:(ci + 1) * 128, :]),
                    waits=cbar, dma=True)
                wmm = []
                for nh2 in range(2):
                    for ic in range(8):
                        wmm.append(p.emit("tensor", lambda e, ic=ic, nh2=nh2: e.matmul(
                            bank[2 + nh2][:, 0:512], ofm[:, ic * 128:(ic + 1) * 128],
                            woa[:, ic * 1024 + nh2 * 512: ic * 1024 + nh2 * 512 + 512],
                            start=(ic == 0), stop=(ic == 7)),
                            waits=[ocp, wo_ev] + pqe))
                res = []
                for nh2 in range(2):
                    res.append(p.emit("vector", lambda e, nh2=nh2: e.tensor_tensor(
                        out=xo_t[:, nh2 * 512:(nh2 + 1) * 512],
                        in0=bank[2 + nh2][:, 0:512],
                        in1=xr2[:, nh2 * 512:(nh2 + 1) * 512], op=ALU.add),
                        waits=wmm[nh2 * 8:nh2 * 8 + 8] + [lx2]))
                p.emit("sync", lambda e, ci=ci: e.dma_start(
                    out=x_dst[ci * 128:(ci + 1) * 128, :], in_=xo_t[:, :]),
                    waits=res, dma=True)
                if DBG["sdump"] and ci == 0:
                    de = p.snap()
                    dumps = [(Vbt[:, :], 0), (Ktm[:, :], 1024), (rhs_s[:, :], 2048),
                             (ytmp[:, :], 3072), (osc[:, :], 4096), (ofm[:, :], 5120),
                             (S0e[:, :], 6144), (Cpa[:, 0:128], 6656),
                             (Ppm[:, 0:128], 6784)]
                    for j in range(7):
                        dumps.append((Bp[j][:, 0:128], 6912 + j * 128))
                    for tsr, off in dumps:
                        p.emit("sync", lambda e, tsr=tsr, off=off: e.dma_start(
                            out=dbg[:, off:off + tsr.shape[-1]], in_=tsr),
                            waits=de, dma=True)

            barrier = p.snap()
        s1.close()
        return p.snap()

    def emit_ffn(x_src, x_dst, tag):
        sf = ExitStack()
        s2 = lambda n, s, d: sf.enter_context(nc.sbuf_tensor(n, s, d))
        x_t = s2(f"x_tF{tag}", [128, 1024], F32)
        stat = s2(f"statF{tag}", [128, 8], F32)
        sq32 = s2(f"sq32F{tag}", [128, 1024], BF16)
        ytm = s2(f"ytmF{tag}", [128, 1024], BF16)
        yfmt = s2(f"yfmtF{tag}", [128, 1024], BF16)
        wrsb = s2(f"wrsbF{tag}", [128, 64], BF16)
        lgt = s2(f"lgtF{tag}", [128, 8], F32)
        rmx = s2(f"rmxF{tag}", [128, 4], F32)
        prb = s2(f"prbF{tag}", [128, 8], F32)
        tops = s2(f"topsF{tag}", [128, 8], F32)
        tidx = s2(f"tidxF{tag}", [128, 8], U32)
        idxf = s2(f"idxfF{tag}", [128, 64], F32)
        gts = s2(f"gtsF{tag}", [128, 64], F32)
        mscr = s2(f"mscrF{tag}", [128, 16], BF16)
        m1em = s2(f"m1emF{tag}", [8, T], BF16)
        m2em = s2(f"m2emF{tag}", [8, T], BF16)
        cs1 = s2(f"cs1F{tag}", [128, T], F32)
        cs2 = s2(f"cs2F{tag}", [128, T], F32)
        gath = s2(f"gathF{tag}", [128, 16], F32)
        bmask = s2(f"bmaskF{tag}", [128, 16], F32)
        brow = s2(f"browF{tag}", [1, 16], F32)
        bcol = s2(f"bcolF{tag}", [8, 4], F32)
        pseltm = s2(f"pseltmF{tag}", [128, 64], F32)
        m1tma = s2(f"m1tmaF{tag}", [128, 256], BF16)
        m2tma = s2(f"m2tmaF{tag}", [128, 256], BF16)
        scr8 = s2(f"scr8F{tag}", [128, 8], F32)
        keep = s2(f"keepF{tag}", [128, 64], F32)
        slotf = s2(f"slotfF{tag}", [128, 64], F32)
        slotu = s2(f"slotuF{tag}", [128, 64], U32)
        bxt = s2(f"bxtF{tag}", [128, 5 * 1024], BF16)
        bxf = s2(f"bxfF{tag}", [128, 8 * 640], BF16)
        h1f = s2(f"h1fF{tag}", [128, 32 * 640], BF16)
        wist = s2(f"wistF{tag}", [128, 2 * 1024], F32)
        wib = s2(f"wibF{tag}", [128, 2 * 1024], BF16)
        wost = s2(f"wostF{tag}", [128, 2 * 512], F32)
        wob = s2(f"wobF{tag}", [128, 2 * 512], BF16)
        o2t = s2(f"o2tF{tag}", [128, 1024], BF16)
        ysc = s2(f"yscF{tag}", [128, 2048], BF16)
        r1 = s2(f"r1F{tag}", [128, 1024], BF16)
        r2 = s2(f"r2F{tag}", [128, 1024], BF16)
        xr2 = s2(f"xr2F{tag}", [128, 1024], F32)
        xo_t = s2(f"xo_tF{tag}", [128, 1024], F32)

        fbar = p.snap()
        z2 = p.emit("vector", lambda e: e.memset(gath[:, :], 0.0), waits=fbar)
        z3 = p.emit("vector", lambda e: e.memset(r1[:, :], 0.0), waits=fbar)
        z4 = p.emit("vector", lambda e: e.memset(r2[:, :], 0.0), waits=fbar)
        wr_l = p.emit("sync", lambda e: e.dma_start(
            out=wist[:, 0:64].rearrange("p (a e) -> p a e", a=8),
            in_=I["w_router"].rearrange("(a p) e -> p a e", p=128)),
            waits=fbar, dma=True)
        wr_c = p.emit("vector", lambda e: e.tensor_copy(
            out=wrsb[:, :], in_=wist[:, 0:64]), waits=[wr_l])

        zb = []
        for i in range(40):
            zb.append(p.emit("gpsimd" if i % 2 else "sync",
                             lambda e, i=i: e.dma_start(
                out=bkt_dram[i * 128:(i + 1) * 128, 0:512], in_=zerbf[:, :]),
                waits=[zb_ev] + fbar, dma=True))
            zb.append(p.emit("gpsimd" if i % 2 else "sync",
                             lambda e, i=i: e.dma_start(
                out=bkt_dram[i * 128:(i + 1) * 128, 512:1024], in_=zerbf[:, :]),
                waits=[zb_ev] + fbar, dma=True))

        ydone = []
        prev_ln = None
        x_buf = [None]
        mtr_prev = []
        for it in range(NT):
            lx = p.emit("sync", lambda e, it=it: e.dma_start(
                out=x_t[:, :], in_=x_src[it * 128:(it + 1) * 128, :]),
                waits=fbar + [x_buf[0]], dma=True)
            yev = ln_tile(x_t[:, :], stat, ytm[:, :], sq32[:, :], [lx], prev_ln)
            x_buf[0] = yev
            yd = p.emit("gpsimd", lambda e, it=it: e.dma_start(
                out=y_dram[it * 128:(it + 1) * 128, :], in_=ytm[:, :]),
                waits=[yev], dma=True)
            ydone.append(yd)
            cp = tm_to_fm(lambda c: ytm[:, c * 128:(c + 1) * 128],
                          yfmt[:, :].rearrange("p (c t) -> p c t", c=8),
                          [yev] + mtr_prev + fbar)
            prev_ln = [cp, yd]
            rmm = []
            for ic in range(8):
                rmm.append(p.emit("tensor", lambda e, ic=ic: e.matmul(
                    bank[0][:, 0:8], yfmt[:, ic * 128:(ic + 1) * 128],
                    wrsb[:, ic * 8:(ic + 1) * 8], start=(ic == 0), stop=(ic == 7)),
                    waits=[cp, wr_c] + mtr_prev))
            lg = p.emit("scalar", lambda e: e.copy(out=lgt[:, :], in_=bank[0][:, 0:8]),
                        waits=rmm)
            r1_ = p.emit("vector", lambda e: e.reduce_max(
                out=rmx[:, 0:1], in_=lgt[:, :], axis=AX.X), waits=[lg])
            r2_ = p.emit("gpsimd", lambda e: e.tensor_scalar(
                out=rmx[:, 1:2], in0=rmx[:, 0:1], scalar1=-1.0, scalar2=None,
                op0=ALU.mult), waits=[r1_])
            r3_ = p.emit("scalar", lambda e: e.activation(
                out=prb[:, :], in_=lgt[:, :], func=AF.Exp, bias=rmx[:, 1:2]),
                waits=[r2_])
            r4_ = p.emit("vector", lambda e: e.reduce_sum(
                out=rmx[:, 2:3], in_=prb[:, :], axis=AX.X), waits=[r3_])
            r5_ = p.emit("vector", lambda e: e.reciprocal(
                out=rmx[:, 3:4], in_=rmx[:, 2:3]), waits=[r4_])
            r6_ = p.emit("gpsimd", lambda e: e.tensor_scalar(
                out=prb[:, :], in0=prb[:, :], scalar1=rmx[:, 3:4], scalar2=None,
                op0=ALU.mult), waits=[r5_])
            t1_ = p.emit("vector", lambda e: (e.max(out=tops[:, :], in_=prb[:, :]),
                                              e.drain())[1], waits=[r6_])
            t2_ = p.emit("vector", lambda e: (e.max_index(
                out=tidx[:, :], in_max=tops[:, :], in_values=prb[:, :]),
                e.drain())[1], waits=[t1_])
            t3_ = p.emit("gpsimd", lambda e, it=it: e.tensor_copy(
                out=idxf[:, it:it + 1], in_=tidx[:, 0:1]), waits=[t2_])
            t3b = p.emit("gpsimd", lambda e, it=it: e.tensor_copy(
                out=idxf[:, 32 + it:33 + it], in_=tidx[:, 1:2]), waits=[t2_])
            g1_ = p.emit("vector", lambda e, it=it: e.tensor_copy(
                out=gts[:, it:it + 1], in_=tops[:, 0:1]), waits=[t1_, t2_])
            g1b = p.emit("vector", lambda e, it=it: e.tensor_copy(
                out=gts[:, 32 + it:33 + it], in_=tops[:, 1:2]), waits=[t1_, t2_])
            m1_ = p.emit("vector", lambda e, it=it: e.tensor_scalar(
                out=mscr[:, 0:8], in0=iota8[:, :], scalar1=idxf[:, it:it + 1],
                scalar2=None, op0=ALU.is_equal), waits=[t3_])
            m2_ = p.emit("vector", lambda e, it=it: e.tensor_scalar(
                out=mscr[:, 8:16], in0=iota8[:, :], scalar1=idxf[:, 32 + it:33 + it],
                scalar2=None, op0=ALU.is_equal), waits=[t3b])
            p.emit("gpsimd", lambda e, it=it: e.tensor_copy(
                out=m1tma[:, it * 8:(it + 1) * 8], in_=mscr[:, 0:8]), waits=[m1_])
            p.emit("gpsimd", lambda e, it=it: e.tensor_copy(
                out=m2tma[:, it * 8:(it + 1) * 8], in_=mscr[:, 8:16]), waits=[m2_])
            tr1 = p.emit("tensor", lambda e: e.matmul(
                bankT[0:8, 0:128], mscr[:, 0:8], ident[:, :],
                is_transpose=True, start=True, stop=False),
                waits=[m1_, cp] + mtr_prev)
            tr2 = p.emit("tensor", lambda e: e.matmul(
                bankT[0:8, 128:256], mscr[:, 8:16], ident[:, :],
                is_transpose=True, start=False, stop=True),
                waits=[m2_, tr1])
            c1_ = p.emit("vector", lambda e, it=it: e.tensor_copy(
                out=m1em[:, it * 128:(it + 1) * 128], in_=bankT[0:8, 0:128]),
                waits=[tr1, tr2])
            c2_ = p.emit("vector", lambda e, it=it: e.tensor_copy(
                out=m2em[:, it * 128:(it + 1) * 128], in_=bankT[0:8, 128:256]),
                waits=[tr2])
            mtr_prev = [c1_, c2_]

        sc1 = p.emit("vector", lambda e: e.tensor_tensor_scan(
            out=cs1[0:8, :], data0=m1em[:, :], data1=m1em[:, :], initial=0.0,
            op0=ALU.add, op1=ALU.max), waits=mtr_prev)
        sc2 = p.emit("vector", lambda e: e.tensor_tensor_scan(
            out=cs2[0:8, :], data0=m2em[:, :], data1=m2em[:, :], initial=0.0,
            op0=ALU.add, op1=ALU.max), waits=[sc1])
        d1 = p.emit("sync", lambda e: e.dma_start(
            out=cc_in[0:1, 0:8], in_=cs1[0:8, T - 1:T]), waits=[sc1], dma=True)
        d2 = p.emit("sync", lambda e: e.dma_start(
            out=cc_in[0:1, 8:16], in_=cs2[0:8, T - 1:T]), waits=[sc2], dma=True)
        cc = p.emit("gpsimd", lambda e: e.collective_compute(
            "AllGather", ALU.bypass, replica_groups=[[0, 1, 2, 3, 4, 5, 6, 7]],
            ins=[cc_in], outs=[cc_out]), waits=[d1, d2] + (cc_prev and [cc_prev[0]] or []))
        gl = p.emit("sync", lambda e: e.dma_start(
            out=gath[0:8, :], in_=cc_out), waits=[cc, z2], dma=True)
        cc_prev.clear()
        cc_prev.append(gl)
        bevs = []
        for bi in range(2):
            bm_ = p.emit("vector", lambda e, bi=bi: e.tensor_tensor(
                out=bmask[:, :], in0=gath[:, :],
                in1=selm[:, bi * 16:(bi + 1) * 16],
                op=ALU.mult), waits=[gl] + bevs)
            bmm_ = p.emit("tensor", lambda e: e.matmul(
                bank[0][0:1, 0:16], onesf[:, 0:1], bmask[:, :],
                start=True, stop=True), waits=[bm_])
            bc_ = p.emit("scalar", lambda e, bi=bi: e.copy(
                out=brow[0:1, 0:16], in_=bank[0][0:1, 0:16]), waits=[bmm_])
            bd_ = p.emit("gpsimd", lambda e, bi=bi: e.dma_start(
                out=bcol[0:8, 2 * bi:2 * bi + 1], in_=brow[0:1, 0:8]),
                waits=[bc_], dma=True)
            bd2_ = p.emit("gpsimd", lambda e, bi=bi: e.dma_start(
                out=bcol[0:8, 2 * bi + 1:2 * bi + 2], in_=brow[0:1, 8:16]),
                waits=[bc_], dma=True)
            bevs = [bd_, bd2_]
        badd = p.emit("vector", lambda e: e.tensor_tensor(
            out=bcol[0:8, 2:3], in0=bcol[0:8, 2:3], in1=bcol[0:8, 3:4], op=ALU.add),
            waits=bevs)
        po1 = p.emit("gpsimd", lambda e: e.tensor_scalar(
            out=cs1[0:8, :], in0=cs1[0:8, :], scalar1=bcol[0:8, 0:1], scalar2=-1.0,
            op0=ALU.add, op1=ALU.add), waits=[badd, sc1])
        po2 = p.emit("gpsimd", lambda e: e.tensor_scalar(
            out=cs2[0:8, :], in0=cs2[0:8, :], scalar1=bcol[0:8, 2:3], scalar2=-1.0,
            op0=ALU.add, op1=ALU.add), waits=[badd, sc2])
        ps_done = []
        tp_prev = [po1, po2]
        for it in range(NT):
            for ki, (cse, mtma) in enumerate(((cs1, m1tma), (cs2, m2tma))):
                bb = 2 + ki
                tp = p.emit("tensor", lambda e, it=it, cse=cse, bb=bb: e.matmul(
                    bank[bb][:, 0:128], cse[:, it * 128:(it + 1) * 128],
                    identf[:, :], is_transpose=True, start=True, stop=True),
                    waits=tp_prev)
                mu = p.emit("vector", lambda e, it=it, mtma=mtma, bb=bb: e.tensor_tensor(
                    out=scr8[:, :], in0=bank[bb][:, 0:8],
                    in1=mtma[:, it * 8:(it + 1) * 8], op=ALU.mult), waits=[tp])
                rd = p.emit("vector", lambda e, it=it, ki=ki: e.reduce_sum(
                    out=pseltm[:, ki * 32 + it: ki * 32 + it + 1], in_=scr8[:, :],
                    axis=AX.X), waits=[mu])
                tp_prev = [mu, rd]
                ps_done.append(rd)
        def kview(tsr, ki):
            return tsr[:, ki * 32:(ki + 1) * 32]
        slc_evs = []
        for ki in range(2):
            k1_ = p.emit("vector", lambda e, ki=ki: e.tensor_single_scalar(
                out=kview(keep, ki), in_=kview(pseltm, ki), scalar=float(CAP),
                op=ALU.is_lt), waits=ps_done)
            g1_ = p.emit("gpsimd", lambda e, ki=ki: e.tensor_tensor(
                out=kview(gts, ki), in0=kview(gts, ki), in1=kview(keep, ki),
                op=ALU.mult), waits=[k1_])
            s1_ = p.emit("vector", lambda e, ki=ki: e.scalar_tensor_tensor(
                out=kview(slotf, ki), in0=kview(idxf, ki), scalar=float(CAP),
                in1=kview(pseltm, ki), op0=ALU.mult, op1=ALU.add),
                waits=ps_done + [k1_])
            inv_ = p.emit("vector", lambda e, ki=ki: e.tensor_scalar(
                out=kview(keep, ki), in0=kview(keep, ki), scalar1=-1.0, scalar2=1.0,
                op0=ALU.mult, op1=ALU.add), waits=[g1_, s1_])
            sb_ = p.emit("vector", lambda e, ki=ki: e.scalar_tensor_tensor(
                out=kview(slotf, ki), in0=kview(keep, ki), scalar=float(2 * NTOT),
                in1=kview(slotf, ki), op0=ALU.mult, op1=ALU.add), waits=[inv_])
            cu_ = p.emit("vector", lambda e, ki=ki: e.tensor_copy(
                out=kview(slotu, ki), in_=kview(slotf, ki)), waits=[sb_])
            slc_evs.append(cu_)

        scat = []
        yre = [None, None]
        for it in range(NT):
            hb = it % 2
            yl = p.emit("sync", lambda e, it=it, hb=hb: e.dma_start(
                out=ysc[:, hb * 1024:(hb + 1) * 1024],
                in_=y_dram[it * 128:(it + 1) * 128, :]),
                waits=[ydone[it], yre[hb]], dma=True)
            for ki in range(2):
                ev_ = p.emit("gpsimd", lambda e, it=it, ki=ki, hb=hb: e.indirect_dma_start(
                    out=bkt_dram,
                    out_offset=IndirectOffsetOnAxis(
                        ap=slotu[:, ki * 32 + it: ki * 32 + it + 1], axis=0),
                    in_=ysc[:, hb * 1024:(hb + 1) * 1024], in_offset=None,
                    bounds_check=bc_reg(e), oob_is_err=False),
                    waits=slc_evs + zb + [yl], dma=True)
                scat.append(ev_)
            yre[hb] = scat[-1]

        o2done = []
        wi_prev = [None, None]
        wo_prev = [None, None]
        wi_cons = [[], []]
        wo_cons = [[], []]
        bx_prev = []
        h1_prev = []
        h1_guard = []
        prev_oevs = []
        for ex in range(E):
            bl = []
            for j in range(5):
                bl.append(p.emit("sync", lambda e, ex=ex, j=j: e.dma_start(
                    out=bxt[:, j * 1024:(j + 1) * 1024],
                    in_=bkt_dram[ex * CAP + j * 128: ex * CAP + (j + 1) * 128, :]),
                    waits=scat + bx_prev, dma=True))
            bf_ = []
            for j in range(5):
                bf_.append(tm_to_fm(
                    lambda c, j=j: bxt[:, j * 1024 + c * 128: j * 1024 + (c + 1) * 128],
                    bxf[:, :].rearrange("p (c ct) -> p c ct", c=8)
                    [:, :, j * 128:(j + 1) * 128],
                    [bl[j]] + h1_prev + (bf_[-1:] if bf_ else list(bx_prev))))
            bx_prev = list(bf_)
            h1_prev = []
            hevs = []
            for fc in range(FF // 128):
                hb = fc % 2
                wl = p.emit("sync", lambda e, ex=ex, fc=fc, hb=hb: e.dma_start(
                    out=wist[:, hb * 1024:(hb + 1) * 1024]
                    .rearrange("p (a b) -> p a b", a=8),
                    in_=I["wi"][ex, :, fc * 128:(fc + 1) * 128]
                    .rearrange("(a p) b -> p a b", p=128)), waits=[wi_prev[hb]], dma=True)
                wc = p.emit("vector", lambda e, hb=hb: e.tensor_copy(
                    out=wib[:, hb * 1024:(hb + 1) * 1024],
                    in_=wist[:, hb * 1024:(hb + 1) * 1024]),
                    waits=[wl] + wi_cons[hb])
                wi_prev[hb] = wc
                wi_cons[hb] = []
                mm1 = []
                for ic in range(8):
                    for nt2 in range(2):
                        w_ = 512 if nt2 == 0 else 128
                        mm1.append(p.emit("tensor",
                                          lambda e, ic=ic, nt2=nt2, hb=hb, w_=w_: e.matmul(
                            bank[nt2][:, 0:w_],
                            wib[:, hb * 1024 + ic * 128:hb * 1024 + (ic + 1) * 128],
                            bxf[:, :].rearrange("p (c ct) -> p c ct", c=8)
                            [:, ic, nt2 * 512:nt2 * 512 + w_],
                            start=(ic == 0), stop=(ic == 7)),
                            waits=[wc] + bf_ + hevs[-2:]))
                wi_cons[hb] = mm1[-16:]
                e1 = p.emit("scalar", lambda e, fc=fc: e.activation(
                    out=h1f[:, fc * 640:fc * 640 + 512], in_=bank[0][:, 0:512],
                    func=AF.Relu), waits=mm1 + h1_guard)
                e2 = p.emit("vector", lambda e, fc=fc: e.tensor_relu(
                    out=h1f[:, fc * 640 + 512:(fc + 1) * 640], in_=bank[1][:, 0:128]),
                    waits=mm1 + h1_guard)
                hevs += [e1, e2]
            h1_prev = hevs[-2:]
            wo_all = []
            for hf in range(2):
                oevs = []
                wo_mm_evs = {ct: [] for ct in range(5)}
                for fc in range(FF // 128):
                    hb = fc % 2
                    wl2 = p.emit("sync", lambda e, ex=ex, fc=fc, hf=hf, hb=hb: e.dma_start(
                        out=wost[:, hb * 512:(hb + 1) * 512],
                        in_=I["wo"][ex, fc * 128:(fc + 1) * 128,
                                    hf * 512:(hf + 1) * 512]),
                        waits=[wo_prev[hb]], dma=True)
                    wc2 = p.emit("vector", lambda e, hb=hb: e.tensor_copy(
                        out=wob[:, hb * 512:(hb + 1) * 512],
                        in_=wost[:, hb * 512:(hb + 1) * 512]),
                        waits=[wl2] + wo_cons[hb])
                    wo_prev[hb] = wc2
                    wo_cons[hb] = []
                    for ct in range(5):
                        wo_mm_evs[ct].append(p.emit("tensor",
                                                    lambda e, fc=fc, ct=ct, hb=hb:
                                                    e.matmul(
                            bank[ct][:, 0:512],
                            h1f[:, fc * 640 + ct * 128:fc * 640 + (ct + 1) * 128],
                            wob[:, hb * 512:(hb + 1) * 512],
                            start=(fc == 0), stop=(fc == 31)),
                            waits=[wc2] + hevs[-2:]
                            + (prev_oevs if fc == 0 else [])))
                    wo_cons[hb] = [wo_mm_evs[ct][-1] for ct in range(5)]
                wo_all += [wo_mm_evs[ct][-1] for ct in range(5)]
                for ct in range(5):
                    ev_ = p.emit("scalar" if ct % 2 == 0 else "vector",
                                 (lambda e, ct=ct: e.copy(
                                     out=o2t[:, 0:512], in_=bank[ct][:, 0:512]))
                                 if ct % 2 == 0 else
                                 (lambda e, ct=ct: e.tensor_copy(
                                     out=o2t[:, 512:1024], in_=bank[ct][:, 0:512])),
                                 waits=wo_mm_evs[ct] + o2done[-2:])
                    half_off = 0 if ct % 2 == 0 else 512
                    od_ = p.emit("gpsimd", lambda e, ex=ex, ct=ct, hf=hf,
                                 half_off=half_off: e.dma_start(
                        out=out2_dram[ex * CAP + ct * 128: ex * CAP + (ct + 1) * 128,
                                      hf * 512:(hf + 1) * 512],
                        in_=o2t[:, half_off:half_off + 512]), waits=[ev_], dma=True)
                    oevs.append(ev_)
                    o2done.append(od_)
                prev_oevs = list(oevs)

            h1_guard = list(wo_all)

        g_prev = []
        for it in range(NT):
            lg1 = p.emit("gpsimd", lambda e, it=it: e.indirect_dma_start(
                out=r1[:, :], out_offset=None, in_=out2_dram,
                in_offset=IndirectOffsetOnAxis(
                    ap=slotu[:, it: it + 1], axis=0),
                bounds_check=bc_reg(e), oob_is_err=False),
                waits=o2done + slc_evs + g_prev, dma=True)
            lg2 = p.emit("gpsimd", lambda e, it=it: e.indirect_dma_start(
                out=r2[:, :], out_offset=None, in_=out2_dram,
                in_offset=IndirectOffsetOnAxis(
                    ap=slotu[:, 32 + it: 33 + it], axis=0),
                bounds_check=bc_reg(e), oob_is_err=False),
                waits=o2done + slc_evs + g_prev, dma=True)
            lx2 = p.emit("sync", lambda e, it=it: e.dma_start(
                out=xr2[:, :], in_=x_src[it * 128:(it + 1) * 128, :]),
                waits=g_prev, dma=True)
            a1 = p.emit("vector", lambda e, it=it: e.scalar_tensor_tensor(
                out=xo_t[:, :], in0=r1[:, :],
                scalar=gts[:, it:it + 1], in1=xr2[:, :],
                op0=ALU.mult, op1=ALU.add), waits=[lg1, lx2])
            a2 = p.emit("vector", lambda e, it=it: e.scalar_tensor_tensor(
                out=xo_t[:, :], in0=r2[:, :],
                scalar=gts[:, 32 + it:33 + it], in1=xo_t[:, :],
                op0=ALU.mult, op1=ALU.add), waits=[lg2, a1])
            sd = p.emit("gpsimd", lambda e, it=it: e.dma_start(
                out=x_dst[it * 128:(it + 1) * 128, :], in_=xo_t[:, :]),
                waits=[a2], dma=True)
            g_prev = [a2, sd]
        if DBG["dump"]:
            de = p.snap()
            p.emit("sync", lambda e: e.dma_start(
                out=dbg[:, 0:1024].bitcast(BF16)[:, 0:1024],
                in_=bkt_dram[0:128, :]), waits=de, dma=True)
            p.emit("sync", lambda e: e.dma_start(
                out=dbg[:, 1024:2048], in_=out2_dram[0:128, :]), waits=de, dma=True)
            p.emit("sync", lambda e: e.dma_start(
                out=dbg[:, 2048:2112].bitcast(F32)[:, 0:32],
                in_=slotu[:, 0:32].bitcast(F32)[:, 0:32]), waits=de, dma=True)
            p.emit("sync", lambda e: e.dma_start(
                out=dbg[:, 2112:2176].bitcast(F32)[:, 0:32],
                in_=pseltm[:, 0:32]), waits=de, dma=True)
            p.emit("sync", lambda e: e.dma_start(
                out=dbg[:, 2176:2240].bitcast(F32)[:, 0:32],
                in_=gts[:, 0:32]), waits=de, dma=True)
        sf.close()
        return p.snap()

    def emit_attn_b(x_src, x_dst):
        sb3 = ExitStack()
        s2 = lambda n, s, d: sb3.enter_context(nc.sbuf_tensor(n, s, d))
        wstage = s2("wstB", [128, 1024], F32)
        wnat = s2("wnatB", [128, 8 * 1024], BF16)
        wvb = s2("wvbB", [128, 8 * 1024], BF16)
        projsb = s2("projB", [128, 8 * 256], BF16)
        wqp = s2("wqpB", [128, 8 * 256], BF16)
        wkp = s2("wkpB", [128, 8 * 256], BF16)
        x_t2 = s2("x_tB", [128, 2 * 1024], F32)
        stat = s2("statB", [128, 8], F32)
        sq32 = s2("sq32B", [128, 1024], BF16)
        ytm_b = s2("ytmB", [128, 4 * 1024], BF16)
        yfm = s2("yfmB", [128, 8 * 512], BF16)
        kp_tm = s2("kptmB", [128, 4 * 256], BF16)
        vtmb = s2("vtmB", [128, 4 * 1024], BF16)
        qp_fm = s2("qpfmB", [128, 2 * 4096], BF16)
        kv_acc = s2("kvaccB", [128, 4 * 512], F32)
        kv_b = s2("kvbB", [128, 4 * 512], BF16)
        fmt = s2("fmtB", [128, 2 * 4 * 512], F32)
        dnm = s2("dnmB", [1, 4096], F32)
        zrow = s2("zrowB", [1, 4096], F32)
        zcols = s2("zcolB", [128, 32], F32)
        xr2 = s2("xr2B", [128, 1024], F32)
        xo_t = s2("xo_tB", [128, 1024], F32)

        bar = p.snap()
        kvz = p.emit("vector", lambda e: e.memset(kv_acc[:, :], 0.0), waits=bar)

        pj_ev = weight_cast(I["projb"], wstage, projsb, 8, 256, waits=bar)
        wq_ev = weight_cast(I["wqbN"], wstage, wnat, 8, 1024, waits=[pj_ev])

        def fold(dst, wev):
            evs = []
            for ft in range(8):
                bk = ft % 2
                mms = []
                for jt in range(8):
                    mms.append(p.emit("tensor", lambda e, ft=ft, jt=jt, bk=bk: e.matmul(
                        bank[bk][:, 0:256],
                        wnat[:, jt * 1024 + ft * 128: jt * 1024 + (ft + 1) * 128],
                        projsb[:, jt * 256:(jt + 1) * 256],
                        start=(jt == 0), stop=(jt == 7)),
                        waits=[wev, pj_ev] + (evs[ft - 2:ft - 1] if ft >= 2 else [])))
                evs.append(p.emit("scalar" if ft % 2 == 0 else "vector",
                                  (lambda e, ft=ft, bk=bk, dst=dst: e.copy(
                                      out=dst[:, ft * 256:(ft + 1) * 256],
                                      in_=bank[bk][:, 0:256]))
                                  if ft % 2 == 0 else
                                  (lambda e, ft=ft, bk=bk, dst=dst: e.tensor_copy(
                                      out=dst[:, ft * 256:(ft + 1) * 256],
                                      in_=bank[bk][:, 0:256])), waits=mms))
            return evs

        q_f = fold(wqp, wq_ev)
        wk_ev = weight_cast(I["wkbN"], wstage, wnat, 8, 1024, waits=[q_f[-1]])
        k_f = fold(wkp, wk_ev)
        wv_ev = weight_cast(I["wvbT"], wstage, wvb, 8, 1024, waits=[wk_ev])

        fmt_st = {0: None, 1: None}
        fmt_i = [0]

        def elu1(psum_ap, dst_ap, mms, w):
            if not DBG["belu"]:
                d_ = p.emit("vector", lambda e: e.tensor_copy(
                    out=dst_ap, in_=psum_ap), waits=mms)
                return d_, [d_]
            i = fmt_i[0] % 2
            fmt_i[0] += 1
            base = i * 2048
            f0 = fmt[:, base:base + w]
            f1 = fmt[:, base + 512:base + 512 + w]
            f2 = fmt[:, base + 1024:base + 1024 + w]
            f3 = fmt[:, base + 1536:base + 1536 + w]
            g = [fmt_st[i]] if fmt_st[i] else []
            t0 = p.emit("scalar", lambda e: e.activation(
                out=f0, in_=psum_ap, func=AF.Identity), waits=mms + g)
            a = p.emit("vector", lambda e: e.tensor_scalar(
                out=f1, in0=f0, scalar1=0.0, scalar2=None, op0=ALU.min),
                waits=[t0] + g)
            c_ = p.emit("gpsimd", lambda e: e.tensor_scalar(
                out=f3, in0=f0, scalar1=0.0, scalar2=None, op0=ALU.max),
                waits=[t0] + g)
            b_ = p.emit("scalar", lambda e: e.activation(
                out=f2, in_=f1, func=AF.Exp), waits=[a] + g)
            d_ = p.emit("vector", lambda e: e.tensor_tensor(
                out=dst_ap, in0=f2, in1=f3, op=ALU.add), waits=[b_, c_])
            fmt_st[i] = d_
            return d_, [t0]

        NCB = DBG["bnc"] if DBG["bnc"] is not None else 8
        lnx = [None, None]
        cp_prev = [None, None, None, None]
        prev_ln = None
        bank_g = {i: [] for i in range(6)}
        bank_g[0] = [k_f[6]]
        bank_g[1] = [k_f[7]]
        kv_mm_last = {qd: None for qd in range(4)}
        kv_add = {qd: None for qd in range(4)}
        kpv_guard = []
        den_prev = []
        den_evs = []
        qp_d_evs = []
        for ch in range(NCB):
            cps = []
            for it in range(4):
                hb = it % 2
                lx = p.emit("sync", lambda e, ch=ch, it=it, hb=hb: e.dma_start(
                    out=x_t2[:, hb * 1024:(hb + 1) * 1024],
                    in_=x_src[(ch * 4 + it) * 128:(ch * 4 + it + 1) * 128, :]),
                    waits=bar + ([lnx[hb]] if lnx[hb] else []), dma=True)
                yev = ln_tile(x_t2[:, hb * 1024:(hb + 1) * 1024], stat,
                              ytm_b[:, it * 1024:(it + 1) * 1024], sq32[:, :],
                              [lx] + ([cp_prev[it]] if cp_prev[it] else []), prev_ln)
                lnx[hb] = yev
                prev_ln = [yev]
                cp = tm_to_fm(lambda c, it=it: ytm_b[:, it * 1024 + c * 128:
                                                    it * 1024 + (c + 1) * 128],
                              yfm[:, :].rearrange("p (c t) -> p c t", c=8)
                              [:, :, it * 128:(it + 1) * 128],
                              [yev] + kpv_guard)
                cp_prev[it] = cp
                cps.append(cp)
            if DBG["bstage"] < 2:
                continue
            for ph in range(2):
                mms = []
                for kc in range(8):
                    mms.append(p.emit("tensor", lambda e, ph=ph, kc=kc, ch=ch: e.matmul(
                        bank[ph][:, 0:512],
                        wqp[:, kc * 256 + ph * 128:kc * 256 + ph * 128 + 128],
                        yfm[:, kc * 512:(kc + 1) * 512],
                        start=(kc == 0), stop=(kc == 7)),
                        waits=cps + [q_f[-1]] + bank_g[ph]))
                d_, bg = elu1(bank[ph][:, 0:512],
                              qp_fm[:, ph * 4096 + ch * 512:ph * 4096 + (ch + 1) * 512],
                              mms, 512)
                bank_g[ph] = bg
                qp_d_evs.append(d_)
            if DBG["bstage"] < 3:
                continue
            kp_d = []
            for it in range(4):
                bk = it % 2
                mms = []
                for kc in range(8):
                    mms.append(p.emit("tensor", lambda e, it=it, kc=kc, bk=bk: e.matmul(
                        bank[bk][:, 0:256],
                        yfm[:, kc * 512 + it * 128:kc * 512 + (it + 1) * 128],
                        wkp[:, kc * 256:(kc + 1) * 256],
                        start=(kc == 0), stop=(kc == 7)),
                        waits=cps + [k_f[-1]] + bank_g[bk]))
                d_, bg = elu1(bank[bk][:, 0:256],
                              kp_tm[:, it * 256:(it + 1) * 256], mms, 256)
                bank_g[bk] = bg
                kp_d.append(d_)
            if DBG["bstage"] < 4:
                continue
            v_evs = []
            for it in range(4):
                for nh in range(2):
                    bk = nh
                    mms = []
                    for kc in range(8):
                        mms.append(p.emit("tensor", lambda e, it=it, nh=nh, kc=kc, bk=bk:
                                          e.matmul(
                            bank[bk][:, 0:512],
                            yfm[:, kc * 512 + it * 128:kc * 512 + (it + 1) * 128],
                            wvb[:, kc * 1024 + nh * 512:kc * 1024 + (nh + 1) * 512],
                            start=(kc == 0), stop=(kc == 7)),
                            waits=cps + [wv_ev] + bank_g[bk]))
                    eng_ = ("scalar", "vector")[(it * 2 + nh) % 2]
                    fn = (lambda e, it=it, nh=nh, bk=bk: e.copy(
                        out=vtmb[:, it * 1024 + nh * 512:it * 1024 + (nh + 1) * 512],
                        in_=bank[bk][:, 0:512])) if eng_ == "scalar" else \
                        (lambda e, it=it, nh=nh, bk=bk: e.tensor_copy(
                            out=vtmb[:, it * 1024 + nh * 512:it * 1024 + (nh + 1) * 512],
                            in_=bank[bk][:, 0:512]))
                    ev_ = p.emit(eng_, fn, waits=mms + kpv_guard)
                    bank_g[bk] = [ev_]
                    v_evs.append(ev_)
            if DBG["bstage"] < 5:
                continue
            new_guard = []
            for ph in range(2):
                for nh in range(2):
                    qd = ph * 2 + nh
                    mms = []
                    for it in range(4):
                        mms.append(p.emit("tensor", lambda e, ph=ph, nh=nh, it=it, qd=qd:
                                          e.matmul(
                            bank[2 + qd][:, 0:512],
                            kp_tm[:, it * 256 + ph * 128:it * 256 + ph * 128 + 128],
                            vtmb[:, it * 1024 + nh * 512:it * 1024 + (nh + 1) * 512],
                            start=(it == 0), stop=(it == 3)),
                            waits=kp_d + v_evs +
                            ([kv_add[qd]] if kv_add[qd] else [kvz])))
                    kv_add[qd] = p.emit("vector",
                                        lambda e, qd=qd: e.tensor_tensor(
                        out=kv_acc[:, qd * 512:(qd + 1) * 512],
                        in0=bank[2 + qd][:, 0:512],
                        in1=kv_acc[:, qd * 512:(qd + 1) * 512], op=ALU.add),
                        waits=[mms[-1], kvz])
                    kv_mm_last[qd] = mms[-1]
                    new_guard.append(kv_mm_last[qd])
            kpv_guard = new_guard
            if DBG["bstage"] < 6:
                continue
            dmm = []
            for ph in range(2):
                dmm.append(p.emit("tensor", lambda e, ph=ph, ch=ch: e.matmul(
                    bankS[0:1, 0:512], onesbf[:, 0:1],
                    qp_fm[:, ph * 4096 + ch * 512:ph * 4096 + (ch + 1) * 512],
                    start=(ph == 0), stop=(ph == 1)),
                    waits=qp_d_evs[-2:] + den_prev))
            de = p.emit("scalar", lambda e, ch=ch: e.copy(
                out=dnm[0:1, ch * 512:(ch + 1) * 512], in_=bankS[0:1, 0:512]),
                waits=dmm)
            den_prev = [de]
            den_evs.append(de)

        if not DBG["bp2"]:
            sb3.close()
            return p.snap()
        za = p.emit("vector", lambda e: e.tensor_scalar(
            out=zrow[0:1, :], in0=dnm[0:1, :], scalar1=1e-8, scalar2=None,
            op0=ALU.add), waits=den_evs)
        zb2 = p.emit("vector", lambda e: e.reciprocal(
            out=zrow[0:1, :], in_=zrow[0:1, :]), waits=[za])
        zc_evs = []
        for tt in range(NT):
            zc_evs.append(p.emit("gpsimd" if tt % 2 else "sync",
                                 lambda e, tt=tt: e.dma_start(
                out=zcols[:, tt:tt + 1], in_=zrow[0:1, tt * 128:(tt + 1) * 128]),
                waits=[zb2], dma=True))
        kvb_evs = []
        for qd in range(4):
            kvb_evs.append(p.emit("vector", lambda e, qd=qd: e.tensor_copy(
                out=kv_b[:, qd * 512:(qd + 1) * 512],
                in_=kv_acc[:, qd * 512:(qd + 1) * 512]), waits=[kv_add[qd]]))

        st_prev = {0: [], 1: []}
        lx_prev = list(bar)
        dma_prev = []
        for tt in range(NT if DBG["bp2s"] >= 2 else 0):
            lx2 = p.emit("sync", lambda e, tt=tt: e.dma_start(
                out=xr2[:, :], in_=x_src[tt * 128:(tt + 1) * 128, :]),
                waits=lx_prev, dma=True)
            s_evs = []
            for nh in range(2):
                mms = []
                for ph in range(2):
                    mms.append(p.emit("tensor", lambda e, tt=tt, nh=nh, ph=ph: e.matmul(
                        bank[nh][:, 0:512],
                        qp_fm[:, ph * 4096 + tt * 128:ph * 4096 + (tt + 1) * 128],
                        kv_b[:, (ph * 2 + nh) * 512:(ph * 2 + nh + 1) * 512],
                        start=(ph == 0), stop=(ph == 1)),
                        waits=kvb_evs + st_prev[nh] + bank_g[nh]))
                bank_g[nh] = []
                if DBG["bp2s"] >= 3:
                    sc_ = p.emit("scalar", lambda e, tt=tt, nh=nh: e.activation(
                        out=xo_t[:, nh * 512:(nh + 1) * 512], in_=bank[nh][:, 0:512],
                        func=AF.Identity, scale=zcols[:, tt:tt + 1]),
                        waits=mms + [zc_evs[tt]] + dma_prev)
                    s_evs.append(p.emit("vector", lambda e, tt=tt, nh=nh: e.tensor_tensor(
                        out=xo_t[:, nh * 512:(nh + 1) * 512],
                        in0=xo_t[:, nh * 512:(nh + 1) * 512],
                        in1=xr2[:, nh * 512:(nh + 1) * 512], op=ALU.add),
                        waits=[sc_, lx2]))
                    st_prev[nh] = [sc_]
                else:
                    s_evs.append(p.emit("vector", lambda e, tt=tt, nh=nh: e.tensor_copy(
                        out=xo_t[:, nh * 512:(nh + 1) * 512], in_=bank[nh][:, 0:512]),
                        waits=mms + [lx2, zc_evs[tt]] + dma_prev))
                    st_prev[nh] = [s_evs[-1]]
            dma_ev = p.emit("gpsimd", lambda e, tt=tt: e.dma_start(
                out=x_dst[tt * 128:(tt + 1) * 128, :], in_=xo_t[:, :]),
                waits=s_evs, dma=True)
            lx_prev = s_evs
            dma_prev = [dma_ev]
        sb3.close()
        return p.snap()

    if DBG["bonly"]:
        emit_attn_b(I["x_in"], out)
        fin = p.snap()
        w = [(s, v) for (s, v, _x) in fin if s.startswith("d") and v > 0]
        p.ops["gpsimd"].append((w, lambda e: e.nop(), "g", 1, False))
        p.val["g"] += 1
        p.finalize()
        return nc
    dst1 = out if n_sub == 1 else xr_a
    emit_attn_a(I["x_in"], dst1)
    if n_sub >= 2:
        emit_ffn(xr_a, out if n_sub == 2 else xr_b, 1)
    if n_sub >= 3:
        emit_attn_b(xr_b, out if n_sub == 3 else xr_a)
    if n_sub >= 4:
        emit_ffn(xr_a, out, 2)

    fin = p.snap()
    w = [(s, v) for (s, v, _x) in fin if s.startswith("d") and v > 0]
    p.ops["gpsimd"].append((w, lambda e: e.nop(), "g", 1, False))
    p.val["g"] += 1

    p.finalize()
    return nc


def host_prep(inputs, core):
    ins = {}
    ins["x_in"] = np.ascontiguousarray(
        np.asarray(inputs["L0_attn_a_x"], dtype=np.float32)[core])
    for nm, w in (("wqaT", "L0_attn_a_Wq"), ("wkaT", "L0_attn_a_Wk"),
                  ("wvaT", "L0_attn_a_Wv"), ("woaT", "L0_attn_a_Wo"),
                  ("wvbT", "L0_attn_b_Wv")):
        ins[nm] = np.ascontiguousarray(np.asarray(inputs[w], dtype=np.float32).T)
    ins["wqbN"] = np.ascontiguousarray(np.asarray(inputs["L0_attn_b_Wq"], dtype=np.float32))
    ins["wkbN"] = np.ascontiguousarray(np.asarray(inputs["L0_attn_b_Wk"], dtype=np.float32))
    ins["wbaT"] = np.ascontiguousarray(
        np.asarray(inputs["L0_attn_a_Wb"], dtype=np.float32).T)
    ins["projb"] = np.ascontiguousarray(
        np.asarray(inputs["L0_attn_b_proj"], dtype=np.float32))
    ins["w_router"] = np.ascontiguousarray(
        np.asarray(inputs["L0_ffn_w_router"], dtype=np.float32))
    ins["wi"] = np.ascontiguousarray(np.asarray(inputs["L0_ffn_wi"], dtype=np.float32))
    ins["wo"] = np.ascontiguousarray(np.asarray(inputs["L0_ffn_wo"], dtype=np.float32))
    cw = np.stack([np.asarray(inputs["L0_attn_a_conv_q"], dtype=np.float32),
                   np.asarray(inputs["L0_attn_a_conv_k"], dtype=np.float32),
                   np.asarray(inputs["L0_attn_a_conv_v"], dtype=np.float32)])
    cw = cw.reshape(3, 8, 128, 4).transpose(2, 0, 1, 3).reshape(128, 96)
    ins["convw"] = np.ascontiguousarray(cw)
    ins["onw_rep"] = np.ascontiguousarray(
        np.tile(np.asarray(inputs["L0_attn_a_o_norm_w"], dtype=np.float32), 2)[:, None])
    bf = ml_dtypes.bfloat16
    ins["ident_bf"] = np.eye(128, dtype=bf)
    ins["ident_f"] = np.eye(128, dtype=np.float32)
    tril = np.tril(np.ones((128, 128), np.float32), -1)
    ins["nmask_sl"] = (-tril).astype(bf)
    ins["nmask_su"] = (-tril.T).astype(bf)
    ins["mask_iu"] = np.triu(np.ones((128, 128), np.float32)).astype(bf)
    ins["iota8"] = np.broadcast_to(np.arange(8, dtype=np.float32), (128, 8)).copy()
    ob = np.zeros((128, 128), np.float32)
    for c in range(8):
        ob[0:64, c * 16 + 2 * c] = 1.0
        ob[64:128, c * 16 + 2 * c + 1] = 1.0
    ins["obk16"] = ob.astype(bf)
    sel = np.zeros((128, 1024), np.float32)
    for c in range(8):
        sel[2 * c, c * 128:c * 128 + 64] = 1.0
        sel[2 * c + 1, c * 128 + 64:c * 128 + 128] = 1.0
    ins["selc"] = sel.astype(bf)
    ins["ones8col"] = np.ones((8, 1), np.float32)
    ins["onesf_c"] = np.ones((128, 1), np.float32)
    ins["zeros128"] = np.zeros((128, 512), np.float32)
    g0 = core // 4 * 4
    m1 = np.zeros((8, 16), np.float32)
    m2 = np.zeros((8, 16), np.float32)
    for c in range(g0, g0 + 4):
        if c < core:
            m1[c, 0:8] = 1.0
            m2[c, 8:16] = 1.0
        m2[c, 0:8] = 1.0
    ins["selm"] = np.concatenate([m1, m2], axis=1)
    return ins


def kernel(**inputs):
    nc = build_program(n_sub=KERNEL_N_SUB)
    in_maps = [host_prep(inputs, c) for c in range(8)]
    res = run_bass_kernel_spmd(nc, in_maps, core_ids=list(range(8)))
    return np.stack([np.asarray(r["out"]) for r in res.results], axis=0)
